# version 20
# baseline (speedup 1.0000x reference)
import math
import numpy as np
import concourse.bass as bass
import concourse.mybir as mybir
from concourse.bass_utils import run_bass_kernel_spmd
from contextlib import ExitStack

F32 = mybir.dt.float32
BF16 = mybir.dt.bfloat16
AF = mybir.ActivationFunctionType
ALU = mybir.AluOpType

D = 2048
SC = 2048
NH = 16
TT = 512
NTT = SC // TT
NCH = D // 128
HB = 64
SCALE = 1.0 / math.sqrt(128.0)
MBIAS = -16.0
EPS = 1e-5
GROUPS = [[0, 1, 2, 3], [4, 5, 6, 7]]
NKC = 64
NWG = 32
PE_SUM_FROM = 0
PHASES = ("ffn1", "attn", "ffn2")
DUMP = ()

DMA_SEMS = ("sy", "gp", "xc", "oa",
            "ca0", "ca1", "ca2", "cb0", "cb1", "cb2",
            "rt0", "rt1", "kh0", "kh1", "vh0", "vh1",
            "qt0", "qt1", "aw0", "aw1", "kv0", "kv1", "ft0", "ft1")
ALL_SEMS = ("pe", "act", "dve", "cc") + DMA_SEMS


class Builder:
    def __init__(self):
        nc = bass.Bass("TRN2", target_bir_lowering=False, debug=False)
        self.nc = nc
        self.ops = {e: [] for e in ("tensor", "scalar", "vector", "sync", "gpsimd")}
        self.cnt = {s: 0 for s in ALL_SEMS}
        self.sems = {}
        self.waited = {e: {} for e in self.ops}
        self.m = {}

        self.xT = nc.dram_tensor("xT", [D, SC], F32, kind="ExternalInput").ap()
        self.upt = nc.dram_tensor("upt", [HB, 128, NCH * 128], F32, kind="ExternalInput").ap()
        self.dnt = nc.dram_tensor("dnt", [NCH, 128, HB * 128], F32, kind="ExternalInput").ap()
        self.wqt = nc.dram_tensor("wqt", [NCH, 128, NCH * 128], F32, kind="ExternalInput").ap()
        self.wkt = nc.dram_tensor("wkt", [NCH, 128, NCH * 128], F32, kind="ExternalInput").ap()
        self.wot = nc.dram_tensor("wot", [NCH, 128, NCH * 128], F32, kind="ExternalInput").ap()
        self.wvt = nc.dram_tensor("wvt", [4, 128, NCH * 512], F32, kind="ExternalInput").ap()
        self.upsc = nc.dram_tensor("upsc", [HB, 128], F32, kind="ExternalInput").ap()
        self.dsc = nc.dram_tensor("dsc", [NCH, 128], F32, kind="ExternalInput").ap()
        self.osc = nc.dram_tensor("osc", [1, 1], F32, kind="ExternalInput").ap()
        self.eye = nc.dram_tensor("eye", [128, 128], F32, kind="ExternalInput").ap()
        self.y = nc.dram_tensor("y", [D, SC], F32, kind="ExternalOutput").ap()

        self.r1 = nc.dram_tensor("r1", [D, SC], F32, kind="Internal").ap()
        self.r2 = nc.dram_tensor("r2", [D, SC], F32, kind="Internal").ap()
        self.upsgn = nc.dram_tensor("upsgn", [HB, 128, NCH * 128], BF16, kind="Internal").ap()
        self.dnsgn = nc.dram_tensor("dnsgn", [NCH, 128, HB * 128], BF16, kind="Internal").ap()
        self.wqb = nc.dram_tensor("wqb", [NCH, 128, NCH * 128], BF16, kind="Internal").ap()
        self.wkb = nc.dram_tensor("wkb", [NCH, 128, NCH * 128], BF16, kind="Internal").ap()
        self.wob = nc.dram_tensor("wob", [NCH, 128, NCH * 128], BF16, kind="Internal").ap()
        self.wvb = nc.dram_tensor("wvb", [4, 128, NCH * 512], BF16, kind="Internal").ap()
        self.qd = nc.dram_tensor("qd", [NH, 128, SC], BF16, kind="Internal").ap()
        self.atd = nc.dram_tensor("atd", [NH, 128, SC], BF16, kind="Internal").ap()
        self.kct = nc.dram_tensor("kct", [D, SC], BF16, kind="Internal")
        self.vct = [nc.dram_tensor(f"vct{i}", [SC, 512], BF16, kind="Internal") for i in range(4)]
        self.kg = nc.dram_tensor("kg", [4 * D, SC], BF16, kind="Internal")
        self.vg = [nc.dram_tensor(f"vg{i}", [4 * SC, 512], BF16, kind="Internal") for i in range(4)]

    def op(self, engine, fn, waits=(), inc=None, amt=None):
        real = []
        for w in waits:
            if w is None:
                continue
            st, v = w
            if v is None or v <= 0:
                continue
            if self.waited[engine].get(st, 0) >= v:
                continue
            self.waited[engine][st] = v
            real.append((self.sems[st], v))
        inc_sem, inc_amt, after = None, 0, None
        if inc is not None:
            inc_amt = amt if amt is not None else (16 if inc in DMA_SEMS else 1)
            self.cnt[inc] += inc_amt
            inc_sem = self.sems[inc]
            after = self.cnt[inc]

        def run(eng, real=real, fn=fn, inc_sem=inc_sem, inc_amt=inc_amt):
            for sem, v in real:
                eng.wait_ge(sem, v)
            ins = fn(eng)
            if inc_sem is not None:
                ins.then_inc(inc_sem, inc_amt)
        self.ops[engine].append(run)
        return after

    def g(self, key, default=0):
        return self.m.get(key, default)

    def build(self):
        nc = self.nc
        with ExitStack() as ctx:
            def sb(name, shape, dt):
                return ctx.enter_context(nc.sbuf_tensor(name, shape, dt))
            s = self
            s.arenaA = sb("arenaA", [128, 32768], BF16)
            s.arenaB = sb("arenaB", [128, 32768], BF16)
            s.rtbuf = sb("rtbuf", [128, 2 * 512], F32)
            s.colA = sb("colA", [128, 3 * 2048], BF16)
            s.colB = sb("colB", [128, 3 * 4096], BF16)
            s.xcast = s.arenaB[:, 16384:24576]
            s.xcast_at = s.arenaA[:, 0:8192]
            s.x2 = sb("x2", [128, 512], BF16)
            s.ring = sb("ring", [128, 10 * 512], BF16)
            s.acc = sb("acc", [128, 512], F32)
            s.att = sb("att", [128, 2 * 512], BF16)
            s.rbs = sb("rbs", [128, 512], F32)
            s.qtl = sb("qtl", [128, 2 * 512], BF16)
            s.kev = s.att
            s.rows = sb("rows", [1, 2560], F32)
            s.onesb = sb("onesb", [128, 1], BF16)
            s.onesf = sb("onesf", [128, 1], F32)
            s.onesr = sb("onesr", [1, 128], F32)
            s.eyesb = sb("eyesb", [128, 128], F32)
            s.upsn = sb("upsn", [HB, 128], F32)
            s.dscn = sb("dscn", [NCH, 128], F32)
            s.upsT = sb("upsT", [128, HB], F32)
            s.dsef = sb("dsef", [128, NCH], F32)
            s.oscb = sb("oscb", [1, 1], F32)
            s.osb = sb("osb", [128, 1], F32)
            s.ftile = sb("ftile", [128, 2 * 512], F32)
            s.consts = sb("consts", [128, 16], F32)

            s.ps_big = ctx.enter_context(nc.psum_tensor("ps_big", [128, 2048], F32))
            s.ps_a0 = ctx.enter_context(nc.psum_tensor("ps_a0", [128, 512], F32))
            s.ps_a1 = ctx.enter_context(nc.psum_tensor("ps_a1", [128, 512], F32))
            s.ps_x = ctx.enter_context(nc.psum_tensor("ps_x", [128, 512], F32))
            s.ps_y = ctx.enter_context(nc.psum_tensor("ps_y", [128, 512], F32))
            s.ps_av = [s.ps_a0, s.ps_a1]

            for st in ALL_SEMS:
                self.sems[st] = ctx.enter_context(nc.semaphore(st))
            block = ctx.enter_context(nc.Block())

            self.prologue()
            if "ffn1" in PHASES:
                only = PHASES == ("ffn1",)
                self.ffn("f1", self.xT, self.y if only else self.r1, first=True)
            if "attn" in PHASES:
                src = self.r1 if "ffn1" in PHASES else self.xT
                only = "ffn2" not in PHASES
                self.qkv(src)
                self.attention()
                self.ophase(src, self.y if only else self.r2)
            if "ffn2" in PHASES:
                src = self.r2 if "attn" in PHASES else (
                    self.r1 if "ffn1" in PHASES else self.xT)
                self.ffn("f2", src, self.y, first=False)

            final = [(st, self.cnt[st]) for st in DMA_SEMS]
            shapes = {"qd": ([NH, 128, SC], BF16), "atd": ([NH, 128, SC], BF16),
                      "kct": ([D, SC], BF16), "vct": ([SC, D], BF16),
                      "kg": ([4 * D, SC], BF16), "vg": ([4 * SC, D], BF16),
                      "r1": ([D, SC], F32), "r2": ([D, SC], F32)}
            for nm in DUMP:
                shp, dt = shapes[nm]
                dump = nc.dram_tensor("dump_" + nm, shp, dt, kind="ExternalOutput").ap()
                srcap = getattr(self, nm)
                if "Handle" in type(srcap).__name__:
                    srcap = srcap.ap()
                def fdmp(e, dump=dump, srcap=srcap):
                    return e.dma_start(out=dump, in_=srcap)
                final = [("sy", self.op("sync", fdmp, waits=final, inc="sy"))] 
            self.op("sync", lambda e: e.nop(), waits=final + [(st, self.cnt[st]) for st in DMA_SEMS])

            @block.sync
            def _(e):
                for f in self.ops["sync"]:
                    f(e)

            @block.gpsimd
            def _(e):
                for f in self.ops["gpsimd"]:
                    f(e)

            @block.tensor
            def _(e):
                for f in self.ops["tensor"]:
                    f(e)

            @block.scalar
            def _(e):
                for f in self.ops["scalar"]:
                    f(e)

            @block.vector
            def _(e):
                for f in self.ops["vector"]:
                    f(e)
        return nc

    def prologue(self):
        s, m = self, self.m
        for idx, val in enumerate([EPS, -0.5, SCALE, MBIAS, -1.0, 0.5, 1.0 / D, 1.0, 0.0]):
            s.op("vector", lambda v, idx=idx, val=val: v.memset(
                s.consts[:, idx:idx + 1], val))
            self.nc.const_aps.register(F32, val, s.consts[:, idx:idx + 1])
        s.op("vector", lambda v: v.memset(s.onesb[:, :], 1.0))
        s.op("vector", lambda v: v.memset(s.onesf[:, :], 1.0))
        m["ones"] = s.op("vector", lambda v: v.memset(s.onesr[:, :], 1.0), inc="dve")
        s.op("sync", lambda e: e.dma_start(out=s.eyesb[:, :], in_=s.eye[:, :]), inc="sy")
        s.op("sync", lambda e: e.dma_start(out=s.dscn[:, :], in_=s.dsc[:, :]), inc="sy")
        s.op("sync", lambda e: e.dma_start(out=s.upsn[:, :], in_=s.upsc[:, :]), inc="sy")
        ld = s.op("sync", lambda e: e.dma_start(out=s.oscb[:, :], in_=s.osc[:, :]), inc="sy")
        m["upsT_mm"] = s.op(
            "tensor",
            lambda t: t.matmul(s.ps_x[:, 0:HB], s.upsn[:, :], s.eyesb[0:64, 0:HB],
                               start=True, stop=True),
            waits=[("sy", ld)], inc="pe")
        m["dsT_mm"] = s.op(
            "tensor",
            lambda t: t.matmul(s.ps_y[:, 0:NCH], s.dscn[:, :], s.eyesb[0:NCH, 0:NCH],
                               start=True, stop=True),
            inc="pe")
        m["upsT"] = s.op("scalar", lambda e: e.copy(s.upsT[:, :], s.ps_x[:, 0:HB]),
                         waits=[("pe", m["upsT_mm"])], inc="act")
        m["dsT"] = s.op("scalar", lambda e: e.copy(s.dsef[:, :], s.ps_y[:, 0:NCH]),
                        waits=[("pe", m["dsT_mm"])], inc="act")
        m["osb_mm"] = s.op(
            "tensor", lambda t: t.matmul(s.ps_x[:, 0:1], s.onesr[:, :], s.oscb[:, :],
                                         start=True, stop=True),
            waits=[("dve", m["ones"]), ("act", m["upsT"])], inc="pe")
        m["osb"] = s.op("scalar", lambda e: e.copy(s.osb[:, :], s.ps_x[:, 0:1]),
                        waits=[("pe", m["osb_mm"])], inc="act")
        m["dsef"] = s.op(
            "vector",
            lambda v: v.tensor_scalar(s.dsef[:, :], s.dsef[:, :], s.osb[:, 0:1], 0.5,
                                      ALU.mult, ALU.mult),
            waits=[("act", m["osb"]), ("act", m["dsT"])], inc="dve")
        m["psx_free"] = ("act", m["osb"])
        m["psy_free"] = ("act", m["dsT"])
        m[("xcast_free", "B")] = ("dve", 0)
        m[("xcast_free", "A")] = ("pe", 0)
        m["x2_free"] = ("pe", 0)
        m["rows_free"] = ("pe", 0)

    def ln_tile(self, tag, tt, src, src_ready=(), xn_war=None, slot=None, xbuf="B"):
        s, m = self, self.m
        slot = tt % 2 if slot is None else slot
        xn = s.arenaB[:, slot * 8192:(slot + 1) * 8192]
        xcast = s.xcast if xbuf == "B" else s.xcast_at
        gpw = [m[("xcast_free", xbuf)]] + list(src_ready)
        for gq in range(4):
            def f(e, gq=gq, tt=tt, src=src):
                return e.dma_start(
                    out=xcast[:, gq * 2048:(gq + 1) * 2048].rearrange("p (c t) -> p c t", c=4),
                    in_=src[gq * 512:(gq + 1) * 512, tt * TT:(tt + 1) * TT]
                        .rearrange("(c p) t -> p c t", p=128))
            xc = s.op("gpsimd", f, waits=gpw if gq == 0 else (), inc="xc")
        for c in range(NCH):
            def f(t, c=c):
                return t.matmul(s.ps_x[0:1, :], s.onesb[:, :],
                                xcast[:, c * 512:(c + 1) * 512],
                                start=(c == 0), stop=(c == NCH - 1), skip_group_check=True)
            r = s.op("tensor", f, waits=[("xc", xc), m["psx_free"]] if c == 0 else (),
                     inc="pe" if c == NCH - 1 else None)
        mm_sum = r
        for c in range(NCH):
            def fs(e, c=c):
                return e.activation(s.x2[:, :], xcast[:, c * 512:(c + 1) * 512],
                                    AF.Square)
            aw = [("xc", xc), m["x2_free"]] + ([m["psy_free"]] if c == 0 else [])
            sq = s.op("scalar", fs, waits=aw, inc="act")
            def f(t, c=c):
                return t.matmul(s.ps_y[0:1, :], s.onesb[:, :], s.x2[:, :],
                                start=(c == 0), stop=(c == NCH - 1), skip_group_check=True)
            r = s.op("tensor", f, waits=[("act", sq)], inc="pe")
            m["x2_free"] = ("pe", r)
        mm_ss = r
        mr = s.rows[0:1, 0:512]
        vr = s.rows[0:1, 512:1024]
        m2 = s.rows[0:1, 1024:1536]
        rr = s.rows[0:1, 1024:1536]
        s.op("vector", lambda v: v.tensor_scalar_mul(mr, s.ps_x[0:1, :], 1.0 / D),
             waits=[("pe", mm_sum), m["rows_free"]])
        s.op("vector", lambda v: v.tensor_scalar_mul(vr, s.ps_y[0:1, :], 1.0 / D),
             waits=[("pe", mm_ss)])
        s.op("vector", lambda v: v.tensor_mul(m2, mr, mr))
        rowv = s.op("vector", lambda v: v.tensor_sub(vr, vr, m2), inc="dve")
        s.op("scalar", lambda e: e.activation(vr, vr, AF.Ln, bias=EPS),
             waits=[("dve", rowv)])
        rowr = s.op("scalar", lambda e: e.activation(rr, vr, AF.Exp, scale=-0.5), inc="act")
        mb = s.op("tensor",
                  lambda t: t.matmul(s.ps_x[:, :], s.onesr[:, :], mr,
                                     start=True, stop=True, skip_group_check=True),
                  waits=[("dve", rowv)], inc="pe")
        rb = s.op("tensor",
                  lambda t: t.matmul(s.ps_y[:, :], s.onesr[:, :], rr,
                                     start=True, stop=True, skip_group_check=True),
                  waits=[("act", rowr)], inc="pe")
        m["rows_free"] = ("pe", rb)
        dw = [("pe", mb), ("pe", self.g(("xnfree", slot))), xn_war]
        for c in range(NCH):
            def f(v, c=c):
                return v.tensor_sub(xn[:, c * 512:(c + 1) * 512],
                                    xcast[:, c * 512:(c + 1) * 512], s.ps_x[:, :])
            r = s.op("vector", f, waits=dw if c == 0 else (),
                     inc="dve" if c == NCH - 1 else None)
        m[("xcast_free", xbuf)] = ("dve", r)
        m["psx_free"] = ("dve", r)
        for c in range(NCH):
            def f(v, c=c):
                return v.tensor_mul(xn[:, c * 512:(c + 1) * 512],
                                    xn[:, c * 512:(c + 1) * 512], s.ps_y[:, :])
            r = s.op("vector", f, waits=[("pe", rb)] if c == 0 else (),
                     inc="dve" if c == NCH - 1 else None)
        m["psy_free"] = ("dve", r)
        m[(tag, "xn", tt)] = r

    def ffn(self, tag, src, dst, first):
        s, m = self, self.m
        self.ln_tile(tag, 0, src, src_ready=self.g((tag, "src_ready", 0), ()),
                     xn_war=self.g("xn_war_" + tag, None))

        def conv_up(j):
            sl = j % 3
            sem = f"ca{sl}"
            csl = s.colA[:, sl * 2048:(sl + 1) * 2048]
            lw = [("pe", self.g(("colA", sl))), (sem, self.g(("wb", sem)))]
            gl = s.op("gpsimd",
                      lambda e, j=j, csl=csl: e.dma_start(out=csl, in_=s.upt[j]),
                      waits=lw, inc=sem)
            sg = s.op("scalar", lambda e, csl=csl: e.activation(csl, csl, AF.Sign),
                      waits=[(sem, gl)], inc="act")
            m[("upwb", j)] = (sem, s.op(
                "sync", lambda e, j=j, csl=csl: e.dma_start(out=s.upsgn[j], in_=csl),
                waits=[("act", sg)], inc=sem))
            m[("wb", sem)] = m[("upwb", j)][1]
            m[("upcv", j)] = ("act", sg)

        def conv_dn(hidx):
            i, hf = divmod(hidx, 2)
            qsl = hidx % 3
            semb = f"cb{qsl}"
            csl = s.colB[:, qsl * 4096:(qsl + 1) * 4096]
            lw = [("pe", self.g(("colB", qsl))), (semb, self.g(("wb", semb)))]
            gl = s.op("gpsimd",
                      lambda e, i=i, hf=hf, csl=csl: e.dma_start(
                          out=csl, in_=s.dnt[i][:, hf * 4096:(hf + 1) * 4096]),
                      waits=lw, inc=semb)
            sg = s.op("scalar",
                      lambda e, csl=csl: e.activation(csl, csl, AF.Sign),
                      waits=[(semb, gl)], inc="act")
            m[("dnwb", i, hf)] = (semb, s.op(
                "sync", lambda e, i=i, hf=hf, csl=csl: e.dma_start(
                    out=s.dnsgn[i][:, hf * 4096:(hf + 1) * 4096], in_=csl),
                waits=[("act", sg)], inc=semb))
            m[("wb", semb)] = m[("dnwb", i, hf)][1]
            m[("dncv", hidx)] = ("act", sg)

        for tt in range(NTT):
            xn = s.arenaB[:, (tt % 2) * 8192:(tt % 2 + 1) * 8192]
            if first and tt == 0:
                conv_up(0)
            for j in range(HB):
                sl = j % 3
                sem = f"ca{sl}"
                csl = s.colA[:, sl * 2048:(sl + 1) * 2048]
                lw = [("pe", self.g(("colA", sl))), (sem, self.g(("wb", sem)))]
                if first and tt == 0:
                    wready = m[("upcv", j)]
                else:
                    r = s.op("sync",
                             lambda e, j=j, csl=csl: e.dma_start(out=csl, in_=s.upsgn[j]),
                             waits=lw + [m[("upwb", j)]], inc=sem)
                    wready = (sem, r)
                pw = [wready, ("dve", m[(tag, "xn", tt)]),
                      self.g(("pp", j % 4), None), ("act", m["upsT"])]
                for k in range(NCH):
                    def f(t, k=k, j=j, csl=csl, xn=xn):
                        return t.matmul(s.ps_big[:, (j % 4) * 512:(j % 4 + 1) * 512],
                                        csl[:, k * 128:(k + 1) * 128],
                                        xn[:, k * 512:(k + 1) * 512],
                                        start=(k == 0), stop=(k == NCH - 1),
                                        skip_group_check=True)
                    r = s.op("tensor", f, waits=pw if k == 0 else (),
                             inc="pe" if k == NCH - 1 else None)
                m[("colA", sl)] = r
                if first and tt == 0 and j + 1 < HB:
                    conv_up(j + 1)
                def fe(e, j=j):
                    return e.activation(s.arenaA[:, j * 512:(j + 1) * 512],
                                        s.ps_big[:, (j % 4) * 512:(j % 4 + 1) * 512],
                                        AF.Relu, scale=s.upsT[:, j:j + 1])
                ev = s.op("scalar", fe,
                          waits=[("pe", r), ("pe", self.g(("arenaA",)))], inc="act")
                m[("pp", j % 4)] = ("act", ev)
            m[("xnfree", tt % 2)] = r
            relu_all = ev
            if tt + 1 < NTT:
                self.ln_tile(tag, tt + 1, src,
                             src_ready=self.g((tag, "src_ready", tt + 1), ()))
            if first and tt == 0:
                conv_dn(0)
                conv_dn(1)
            for i in range(NCH):
                rsl = i % 2
                def fr(e, i=i, tt=tt, rsl=rsl, src=src):
                    return e.dma_start(
                        out=s.rtbuf[:, rsl * 512:(rsl + 1) * 512],
                        in_=src[i * 128:(i + 1) * 128, tt * TT:(tt + 1) * TT])
                rtw = [("dve", self.g(("rtbuf", rsl)))] + \
                    list(self.g((tag, "src_ready", tt), ()))
                rt = s.op("gpsimd", fr, waits=rtw, inc=f"rt{rsl}")
                gw = [("act", relu_all), self.g(("pp", i % 4), None)]
                for hf in range(2):
                    qsl = (i * 2 + hf) % 3
                    semb = f"cb{qsl}"
                    csl = s.colB[:, qsl * 4096:(qsl + 1) * 4096]
                    lw = [("pe", self.g(("colB", qsl))), (semb, self.g(("wb", semb)))]
                    if first and tt == 0:
                        wready = m[("dncv", i * 2 + hf)]
                    else:
                        r = s.op("sync",
                                 lambda e, i=i, hf=hf, csl=csl: e.dma_start(
                                     out=csl, in_=s.dnsgn[i][:, hf * 4096:(hf + 1) * 4096]),
                                 waits=lw + [m[("dnwb", i, hf)]], inc=semb)
                        wready = (semb, r)
                    for k4 in range(32):
                        c = hf * 32 + k4
                        def f(t, k4=k4, c=c, i=i, csl=csl):
                            return t.matmul(
                                s.ps_big[:, (i % 4) * 512:(i % 4 + 1) * 512],
                                csl[:, k4 * 128:(k4 + 1) * 128],
                                s.arenaA[:, c * 512:(c + 1) * 512],
                                start=(c == 0), stop=(c == HB - 1),
                                skip_group_check=True)
                        w = (gw + [wready]) if k4 == 0 else ()
                        r = s.op("tensor", f, waits=w,
                                 inc="pe" if k4 == 31 else None)
                    m[("colB", qsl)] = r
                if first and tt == 0:
                    for nh in (i * 2 + 2, i * 2 + 3):
                        if nh < 2 * NCH:
                            conv_dn(nh)
                def fa(v, i=i, rsl=rsl):
                    return v.scalar_tensor_tensor(
                        s.ftile[:, (i % 2) * 512:(i % 2 + 1) * 512],
                        s.ps_big[:, (i % 4) * 512:(i % 4 + 1) * 512],
                        s.dsef[:, i:i + 1],
                        s.rtbuf[:, rsl * 512:(rsl + 1) * 512],
                        ALU.mult, ALU.add)
                ra = s.op("vector", fa,
                          waits=[("pe", r), (f"ft{i % 2}", self.g(("ftw", i % 2))),
                                 ("dve", m["dsef"]), (f"rt{rsl}", rt)], inc="dve")
                m[("pp", i % 4)] = ("dve", ra)
                m[("rtbuf", rsl)] = ra
                def fw(e, i=i, tt=tt, dst=dst):
                    return e.dma_start(out=dst[i * 128:(i + 1) * 128, tt * TT:(tt + 1) * TT],
                                       in_=s.ftile[:, (i % 2) * 512:(i % 2 + 1) * 512])
                m[("ftw", i % 2)] = s.op("gpsimd", fw, waits=[("dve", ra)], inc=f"ft{i % 2}")
            m[("arenaA",)] = r
            m[(tag, "dnw", tt)] = (("ft0", self.cnt["ft0"]), ("ft1", self.cnt["ft1"]))

    def qkv(self, src):
        s, m = self, self.m
        if ("f1", "dnw", 0) in m:
            for tt in range(NTT):
                m[("at", "src_ready", tt)] = m[("f1", "dnw", tt)]
        m[("xcast_free", "A")] = ("pe", self.g(("arenaA",)))
        for tt in range(NTT):
            war = m[("xcast_free", "B")] if tt == 2 else None
            self.ln_tile("at", tt, src, src_ready=self.g(("at", "src_ready", tt), ()),
                         slot=tt, xbuf="B" if tt < 2 else "A", xn_war=war)

        def emit_kq(which):
            wt = s.wkt if which == "k" else s.wqt
            for j in range(NCH):
                sl = j % 3
                sem = f"ca{sl}"
                csl = s.colA[:, sl * 2048:(sl + 1) * 2048]
                gl = s.op("gpsimd",
                          lambda e, j=j, csl=csl, wt=wt: e.dma_start(out=csl, in_=wt[j]),
                          waits=[("pe", self.g(("colA", sl)))], inc=sem)
                for tt in range(NTT):
                    xn = s.arenaB[:, tt * 8192:(tt + 1) * 8192]
                    bank = (j * 4 + tt) % 4
                    pw = [(sem, gl), ("dve", m[("at", "xn", tt)]),
                          self.g(("pp", bank), None)]
                    for k in range(NCH):
                        def f(t, k=k, csl=csl, xn=xn, bank=bank):
                            return t.matmul(
                                s.ps_big[:, bank * 512:(bank + 1) * 512],
                                csl[:, k * 128:(k + 1) * 128],
                                xn[:, k * 512:(k + 1) * 512],
                                start=(k == 0), stop=(k == NCH - 1),
                                skip_group_check=True)
                        r = s.op("tensor", f, waits=pw if k == 0 else (),
                                 inc="pe" if k == NCH - 1 else None)
                    m[("colA", sl)] = r
                    esl = (j * 4 + tt) % 2
                    def fe(e, bank=bank, esl=esl):
                        return e.activation(s.kev[:, esl * 512:(esl + 1) * 512],
                                            s.ps_big[:, bank * 512:(bank + 1) * 512],
                                            AF.Copy)
                    a = s.op("scalar", fe,
                             waits=[("pe", r), (f"kv{esl}", self.g(("kevw", esl)))],
                             inc="act")
                    m[("pp", bank)] = ("act", a)
                    if which == "k":
                        def fw(e, j=j, tt=tt, esl=esl):
                            return e.dma_start(
                                out=s.kct.ap()[j * 128:(j + 1) * 128, tt * TT:(tt + 1) * TT],
                                in_=s.kev[:, esl * 512:(esl + 1) * 512])
                    else:
                        def fw(e, j=j, tt=tt, esl=esl):
                            return e.dma_start(
                                out=s.qd[j][:, tt * TT:(tt + 1) * TT],
                                in_=s.kev[:, esl * 512:(esl + 1) * 512])
                    m[("kevw", esl)] = s.op("sync", fw, waits=[("act", a)],
                                            inc=f"kv{esl}")
                    if which == "q":
                        m[("qw", j, tt)] = (f"kv{esl}", m[("kevw", esl)])

        def emit_v():
            for sl4 in range(4):
                rdy = []
                for h2 in range(2):
                    qsl = (sl4 * 2 + h2) % 3
                    semb = f"cb{qsl}"
                    csl = s.colB[:, qsl * 4096:(qsl + 1) * 4096]
                    gl = s.op("gpsimd",
                              lambda e, sl4=sl4, h2=h2, csl=csl: e.dma_start(
                                  out=csl, in_=s.wvt[sl4][:, h2 * 4096:(h2 + 1) * 4096]),
                              waits=[("pe", self.g(("colB", qsl)))], inc=semb)
                    rdy.append((semb, gl))
                    m[("vcsl", h2)] = csl
                    m[("vqsl", h2)] = qsl
                for idx in range(16):
                    tt, tb = idx // 4, idx % 4
                    xn = s.arenaB[:, tt * 8192:(tt + 1) * 8192]
                    bank = idx % 4
                    pw = rdy + [("dve", m[("at", "xn", tt)]),
                                self.g(("pp", bank), None)]
                    for k in range(NCH):
                        csl = m[("vcsl", k // 8)]
                        def f(t, k=k, csl=csl, xn=xn, tb=tb, bank=bank):
                            return t.matmul(
                                s.ps_big[:, bank * 512:(bank + 1) * 512],
                                xn[:, k * 512 + tb * 128:k * 512 + (tb + 1) * 128],
                                csl[:, (k % 8) * 512:(k % 8 + 1) * 512],
                                start=(k == 0), stop=(k == NCH - 1),
                                skip_group_check=True)
                        r = s.op("tensor", f, waits=pw if k == 0 else (),
                                 inc="pe" if k == NCH - 1 else None)
                    for h2 in range(2):
                        m[("colB", m[("vqsl", h2)])] = r
                    esl = idx % 2
                    def fe(e, bank=bank, esl=esl):
                        return e.activation(s.kev[:, esl * 512:(esl + 1) * 512],
                                            s.ps_big[:, bank * 512:(bank + 1) * 512],
                                            AF.Copy)
                    a = s.op("scalar", fe,
                             waits=[("pe", r), (f"kv{esl}", self.g(("kevw", esl)))],
                             inc="act")
                    m[("pp", bank)] = ("act", a)
                    def fw(e, tb=tb, tt=tt, sl4=sl4, esl=esl):
                        return e.dma_start(
                            out=s.vct[sl4].ap()[tt * 512 + tb * 128:tt * 512 + (tb + 1) * 128, :],
                            in_=s.kev[:, esl * 512:(esl + 1) * 512])
                    m[("kevw", esl)] = s.op("sync", fw, waits=[("act", a)],
                                            inc=f"kv{esl}")
                kv_tot = [("kv0", self.cnt["kv0"]), ("kv1", self.cnt["kv1"])]
                s.op("gpsimd",
                     lambda e, sl4=sl4: e.collective_compute(
                         "AllGather", ALU.bypass, replica_groups=GROUPS,
                         ins=[s.vct[sl4].ap().opt()],
                         outs=[s.vg[sl4].ap().opt()]),
                     waits=kv_tot, inc="cc", amt=1)

        emit_kq("k")
        kv_tot = [("kv0", self.cnt["kv0"]), ("kv1", self.cnt["kv1"])]
        s.op("gpsimd",
             lambda e: e.collective_compute(
                 "AllGather", ALU.bypass, replica_groups=GROUPS,
                 ins=[s.kct.ap().opt()], outs=[s.kg.ap().opt()]),
             waits=kv_tot, inc="cc", amt=1)
        emit_v()
        emit_kq("q")
        m["q_done"] = [("kv0", self.cnt["kv0"]), ("kv1", self.cnt["kv1"])]
        m["qkv_pe_done"] = self.cnt["pe"]
        m[("arenaA",)] = self.cnt["pe"]

    def attention(self):
        s, m = self, self.m
        for j in range(NCH):
            sl = j % 3
            semw = f"ca{sl}"
            csl = s.colA[:, sl * 2048:(sl + 1) * 2048]
            gl = s.op("gpsimd",
                      lambda e, j=j, csl=csl: e.dma_start(out=csl, in_=s.wot[j]),
                      waits=[("pe", self.g(("colA", sl))), (semw, self.g(("wb", semw)))],
                      inc=semw)
            m[("colA", sl)] = self.g(("colA", sl))
            m[("wowb", j)] = (semw, s.op(
                "sync", lambda e, j=j, csl=csl, semw=semw: e.dma_start(out=s.wob[j], in_=csl),
                waits=[(semw, gl)], inc=semw))
            m[("wb", semw)] = m[("wowb", j)][1]
        ra = s.rows[0:1, 1536:2048]
        rb = s.rows[0:1, 2048:2560]
        for h in range(NH):
            hs = h % 2
            kh = s.arenaB[:, hs * 8192:(hs + 1) * 8192]
            vh = s.arenaB[:, 16384 + hs * 8192:16384 + (hs + 1) * 8192]
            khw = [("cc", 1), ("pe", self.g(("khfree", hs), m["qkv_pe_done"]))]
            for rk in range(4):
                def f(e, rk=rk, h=h, kh=kh):
                    return e.dma_start(
                        out=kh[:, rk * 2048:(rk + 1) * 2048],
                        in_=s.kg.ap()[rk * D + h * 128:rk * D + (h + 1) * 128, :])
                kh_rdy = s.op("sync", f, waits=khw if rk == 0 else (), inc=f"kh{hs}")
            vhw = [("cc", 2 + h // 4), ("pe", self.g(("vhfree", hs), m["qkv_pe_done"])),
                   m[("xcast_free", "B")] if hs == 0 else None]
            for g8 in range(8):
                def f(e, g8=g8, h=h, vh=vh):
                    return e.dma_start(
                        out=vh[:, g8 * 1024:(g8 + 1) * 1024].rearrange("p (c v) -> p c v", c=8),
                        in_=s.vg[h // 4].ap()[g8 * 1024:(g8 + 1) * 1024,
                                              (h % 4) * 128:(h % 4 + 1) * 128]
                            .rearrange("(c p) v -> p c v", p=128))
                vh_rdy = s.op("sync", f, waits=vhw if g8 == 0 else (), inc=f"vh{hs}")
            for qt in range(NTT):
                pi = h * NTT + qt
                qsl = qt % 2
                def fq(e, h=h, qt=qt, qsl=qsl):
                    return e.dma_start(out=s.qtl[:, qsl * 512:(qsl + 1) * 512],
                                       in_=s.qd[h][:, qt * TT:(qt + 1) * TT])
                qrd = s.op("sync", fq,
                           waits=[m[("qw", h, qt)],
                                  ("pe", self.g(("qtl", qsl), m["qkv_pe_done"]))],
                           inc=f"qt{qsl}")
                avp = s.ps_av[pi % 2]

                def emit_sc(gg):
                    gp2 = gg % 2
                    scw = [(f"kh{hs}", kh_rdy), (f"qt{qsl}", qrd),
                           self.g(("pp", 2 * gp2), None),
                           self.g(("pp", 2 * gp2 + 1), None)]
                    for u in range(2):
                        c = 2 * gg + u
                        def f(t, c=c, u=u, gp2=gp2, kh=kh, qsl=qsl):
                            return t.matmul(
                                s.ps_big[:, gp2 * 1024 + u * 512:gp2 * 1024 + (u + 1) * 512],
                                kh[:, c * 128:(c + 1) * 128],
                                s.qtl[:, qsl * 512:(qsl + 1) * 512],
                                start=True, stop=True, skip_group_check=True)
                        r = s.op("tensor", f, waits=scw if u == 0 else (),
                                 inc="pe" if u == 1 else None)
                    rsl = gg % 5
                    exw = [("pe", r), ("pe", self.g(("ringpe", rsl))),
                           ("dve", self.g(("ringdve", rsl)))]
                    def fe(e, gp2=gp2, rsl=rsl):
                        return e.activation(s.ring[:, rsl * 1024:(rsl + 1) * 1024],
                                            s.ps_big[:, gp2 * 1024:(gp2 + 1) * 1024],
                                            AF.Exp, bias=MBIAS, scale=SCALE)
                    a = s.op("scalar", fe, waits=exw, inc="act")
                    m[("pp", 2 * gp2)] = ("act", a)
                    m[("pp", 2 * gp2 + 1)] = ("act", a)
                    m[("ex", gg)] = a

                def emit_avsum(gg):
                    rsl = gg % 5
                    a = m[("ex", gg)]
                    avw = [("act", a), (f"vh{hs}", vh_rdy),
                           ("dve", self.g(("avfree", pi % 2)))]
                    for u in range(2):
                        c = 2 * gg + u
                        def f(t, c=c, u=u, rsl=rsl, vh=vh, avp=avp):
                            return t.matmul(avp[:, :], vh[:, c * 128:(c + 1) * 128],
                                            s.ring[:, (rsl * 2 + u) * 512:(rsl * 2 + u + 1) * 512],
                                            start=(c == 0), stop=(c == NKC - 1),
                                            skip_group_check=True)
                        r = s.op("tensor", f, waits=avw if u == 0 else (),
                                 inc="pe" if u == 1 else None)
                    m[("ringpe", rsl)] = r
                    if gg % 4 == 3:
                        fw_ = [("act", a)]
                        if gg == 3:
                            fw_.append(("dve", self.g(("psyfree_at", pi - 1))))
                            if pi == 0:
                                fw_.append(m["psy_free"])
                        for u in range(2):
                            def f(t, rsl=rsl, u=u, gg=gg):
                                return t.matmul(
                                    s.ps_y[0:1, :], s.onesb[:, :],
                                    s.ring[:, (rsl * 2 + u) * 512:(rsl * 2 + u + 1) * 512],
                                    start=(gg == 3 and u == 0), stop=False,
                                    skip_group_check=True)
                            r = s.op("tensor", f, waits=fw_ if u == 0 else (),
                                     inc="pe" if u == 1 else None)
                        m[("ringpe", rsl)] = r
                        return None
                    dw = [("act", a)]
                    if gg == 0:
                        dw.append(("pe", self.g(("fold", pi - 1))))
                        def f(v, rsl=rsl):
                            return v.tensor_add(
                                s.acc[:, :],
                                s.ring[:, (rsl * 2) * 512:(rsl * 2 + 1) * 512],
                                s.ring[:, (rsl * 2 + 1) * 512:(rsl * 2 + 2) * 512])
                        rd = s.op("vector", f, waits=dw, inc="dve")
                    else:
                        for u in range(2):
                            def f(v, rsl=rsl, u=u):
                                return v.tensor_add(
                                    s.acc[:, :], s.acc[:, :],
                                    s.ring[:, (rsl * 2 + u) * 512:(rsl * 2 + u + 1) * 512])
                            rd = s.op("vector", f, waits=dw if u == 0 else (),
                                     inc="dve" if u == 1 else None)
                    m[("ringdve", rsl)] = rd
                    return rd

                DELAY = 4
                accr = None
                for gg in range(NWG):
                    emit_sc(gg)
                    if gg >= DELAY:
                        rd = emit_avsum(gg - DELAY)
                        accr = rd if rd is not None else accr
                for gg in range(NWG - DELAY, NWG):
                    rd = emit_avsum(gg)
                    accr = rd if rd is not None else accr
                m[("qtl", qsl)] = self.cnt["pe"]
                tailw = [("dve", accr)]
                def ff(t):
                    return t.matmul(s.ps_y[0:1, :], s.onesf[:, :], s.acc[:, :],
                                    start=False, stop=True, skip_group_check=True)
                fold = s.op("tensor", ff, waits=tailw, inc="pe")
                m[("fold", pi)] = fold
                s.op("scalar", lambda e: e.activation(ra, s.ps_y[0:1, :], AF.Ln),
                     waits=[("pe", fold), ("pe", self.g(("rbmm", pi - 1)))])
                rcp = s.op("scalar", lambda e: e.activation(rb, ra, AF.Exp, scale=-1.0),
                           inc="act")
                rbmm = s.op("tensor",
                            lambda t: t.matmul(s.ps_y[:, :], s.onesr[:, :], rb,
                                               start=True, stop=True, skip_group_check=True),
                            waits=[("act", rcp)], inc="pe")
                m[("rbmm", pi)] = rbmm
                def fc(v, qsl=qsl):
                    return v.tensor_copy(s.rbs[:, 0:512], s.ps_y[:, :])
                rcpr = s.op("vector", fc, waits=[("pe", rbmm)], inc="dve")
                m[("psyfree_at", pi)] = rcpr
                def fm2(v, qsl=qsl, avp=avp):
                    return v.tensor_mul(s.att[:, qsl * 512:(qsl + 1) * 512],
                                        avp[:, :], s.rbs[:, 0:512])
                attm = s.op("vector", fm2,
                            waits=[("pe", m[("ringpe", (NWG - 1) % 4)]),
                                   ("dve", rcpr),
                                   (f"aw{qsl}", self.g(("atw_s", qsl)))],
                            inc="dve")
                m[("avfree", pi % 2)] = attm
                def fw2(e, h=h, qt=qt, qsl=qsl):
                    return e.dma_start(out=s.atd[h][:, qt * TT:(qt + 1) * TT],
                                       in_=s.att[:, qsl * 512:(qsl + 1) * 512])
                m[("atw_s", qsl)] = s.op("gpsimd", fw2, waits=[("dve", attm)], inc=f"aw{qsl}")
                m[("atw", pi)] = (f"aw{qsl}", m[("atw_s", qsl)])
            m[("khfree", hs)] = self.cnt["pe"]
            m[("vhfree", hs)] = self.cnt["pe"]
        m["attn_pe_done"] = self.cnt["pe"]
        m["psy_free"] = ("dve", m[("psyfree_at", NH * NTT - 1)])
        m[("xnfree", 0)] = m["attn_pe_done"]
        m[("xnfree", 1)] = m["attn_pe_done"]
        m[("xcast_free", "B")] = ("pe", m["attn_pe_done"])

    def ophase(self, rsrc, dst):
        s, m = self, self.m
        for tt in range(NTT):
            for k in range(NCH):
                def f(e, k=k, tt=tt):
                    return e.dma_start(out=s.arenaA[:, k * 512:(k + 1) * 512],
                                       in_=s.atd[k][:, tt * TT:(tt + 1) * TT])
                w = [m[("atw", k * NTT + tt)], ("pe", self.g(("arenaA",)))]
                oa = s.op("sync", f, waits=w, inc="oa")
            for j in range(NCH):
                sl = j % 2
                sem = f"ca{sl}"
                csl = s.colA[:, sl * 2048:(sl + 1) * 2048]
                lw = [("pe", self.g(("colA", sl))), m[("wowb", j)]]
                wrd = s.op("sync",
                           lambda e, j=j, csl=csl: e.dma_start(out=csl, in_=s.wob[j]),
                           waits=lw, inc=sem)
                pw = [(sem, wrd), ("oa", oa), self.g(("pp", j % 4), None)]
                for k in range(NCH):
                    def f(t, k=k, j=j, csl=csl):
                        return t.matmul(s.ps_big[:, (j % 4) * 512:(j % 4 + 1) * 512],
                                        csl[:, k * 128:(k + 1) * 128],
                                        s.arenaA[:, k * 512:(k + 1) * 512],
                                        start=(k == 0), stop=(k == NCH - 1),
                                        skip_group_check=True)
                    r = s.op("tensor", f, waits=pw if k == 0 else (),
                             inc="pe" if k == NCH - 1 else None)
                m[("colA", sl)] = r
                rsl = j % 2
                def frt(e, j=j, tt=tt, rsl=rsl, rsrc=rsrc):
                    return e.dma_start(out=s.rtbuf[:, rsl * 512:(rsl + 1) * 512],
                                       in_=rsrc[j * 128:(j + 1) * 128, tt * TT:(tt + 1) * TT])
                rtw = [("dve", self.g(("rtbuf", rsl)))] + \
                    list(self.g(("f1", "dnw", tt), ()))
                rt = s.op("gpsimd", frt, waits=rtw, inc=f"rt{rsl}")
                def fa(v, j=j, rsl=rsl):
                    return v.tensor_add(s.ftile[:, (j % 2) * 512:(j % 2 + 1) * 512],
                                        s.ps_big[:, (j % 4) * 512:(j % 4 + 1) * 512],
                                        s.rtbuf[:, rsl * 512:(rsl + 1) * 512])
                ra2 = s.op("vector", fa,
                           waits=[("pe", r), (f"ft{j % 2}", self.g(("ftw", j % 2))),
                                  (f"rt{rsl}", rt)], inc="dve")
                m[("pp", j % 4)] = ("dve", ra2)
                m[("rtbuf", rsl)] = ra2
                def fw(e, j=j, tt=tt, dst=dst):
                    return e.dma_start(out=dst[j * 128:(j + 1) * 128, tt * TT:(tt + 1) * TT],
                                       in_=s.ftile[:, (j % 2) * 512:(j % 2 + 1) * 512])
                m[("ftw", j % 2)] = s.op("gpsimd", fw, waits=[("dve", ra2)], inc=f"ft{j % 2}")
            m[("arenaA",)] = r
            m[("f2", "src_ready", tt)] = (("ft0", self.cnt["ft0"]),
                                          ("ft1", self.cnt["ft1"]))
        m["xn_war_f2"] = ("pe", self.g("attn_pe_done"))


def host_layout(inputs):
    x = np.asarray(inputs["L0_attn_x"], np.float32)

    def lhsT_tiles(W, J, K):
        W = np.asarray(W, np.float32)
        return np.ascontiguousarray(
            W.reshape(J, 128, K, 128).transpose(0, 3, 2, 1).reshape(J, 128, K * 128))

    wq = lhsT_tiles(inputs["L0_attn_w_q"], NCH, NCH)
    wk = lhsT_tiles(inputs["L0_attn_w_k"], NCH, NCH)
    wo = lhsT_tiles(inputs["L0_attn_w_o"], NCH, NCH)
    up = lhsT_tiles(inputs["L0_ffn_up_weight"], HB, NCH)
    dn = lhsT_tiles(inputs["L0_ffn_down_weight"], NCH, HB)
    wv_ = np.asarray(inputs["L0_attn_w_v"], np.float32)
    wv = np.ascontiguousarray(
        wv_.reshape(4, 512, NCH, 128).transpose(0, 3, 2, 1).reshape(4, 128, NCH * 512))
    upsc = np.ascontiguousarray(
        np.asarray(inputs["L0_ffn_up_scale"], np.float32).reshape(HB, 128))
    dsc = np.ascontiguousarray(
        np.asarray(inputs["L0_ffn_down_scale"], np.float32).reshape(NCH, 128))
    osc = np.asarray(inputs["L0_ffn_output_scale"], np.float32).reshape(1, 1)
    eye = np.eye(128, dtype=np.float32)
    shared = dict(upt=up, dnt=dn, wqt=wq, wkt=wk, wot=wo, wvt=wv,
                  upsc=upsc, dsc=dsc, osc=osc, eye=eye)
    in_maps = []
    for c in range(8):
        b, r = c // 4, c % 4
        in_maps.append(dict(shared, xT=np.ascontiguousarray(x[b, r * SC:(r + 1) * SC, :].T)))
    return in_maps


def kernel(**inputs):
    in_maps = host_layout(inputs)
    nc = Builder().build()
    res = run_bass_kernel_spmd(nc, in_maps, core_ids=list(range(8)))
    out = np.empty((2, 8192, D), np.float32)
    for c in range(8):
        b, r = c // 4, c % 4
        out[b, r * SC:(r + 1) * SC, :] = res.results[c]["y"].T
    return out


# version 21 (final)
# speedup vs baseline: 1.0679x; 1.0679x over previous
import math
import numpy as np
import concourse.bass as bass
import concourse.mybir as mybir
from concourse.bass_utils import run_bass_kernel_spmd
from contextlib import ExitStack

F32 = mybir.dt.float32
BF16 = mybir.dt.bfloat16
AF = mybir.ActivationFunctionType
ALU = mybir.AluOpType

D = 2048
SC = 2048
NH = 16
TT = 512
NTT = SC // TT
NCH = D // 128
HB = 64
SCALE = 1.0 / math.sqrt(128.0)
MBIAS = -16.0
EPS = 1e-5
GROUPS = [[0, 1, 2, 3], [4, 5, 6, 7]]
NKC = 64
NWG = 32
PE_SUM_FROM = 0
PHASES = ("ffn1", "attn", "ffn2")
DUMP = ()

DMA_SEMS = ("sy", "gp", "xc", "oa",
            "ca0", "ca1", "ca2", "cb0", "cb1", "cb2",
            "rt0", "rt1", "kh0", "kh1", "vh0", "vh1",
            "qt0", "qt1", "aw0", "aw1", "kv0", "kv1", "ft0", "ft1")
ALL_SEMS = ("pe", "act", "dve", "cc") + DMA_SEMS


class Builder:
    def __init__(self):
        nc = bass.Bass("TRN2", target_bir_lowering=False, debug=False)
        self.nc = nc
        self.ops = {e: [] for e in ("tensor", "scalar", "vector", "sync", "gpsimd")}
        self.cnt = {s: 0 for s in ALL_SEMS}
        self.sems = {}
        self.waited = {e: {} for e in self.ops}
        self.m = {}

        self.xT = nc.dram_tensor("xT", [D, SC], F32, kind="ExternalInput").ap()
        self.upt = nc.dram_tensor("upt", [HB, 128, NCH * 128], F32, kind="ExternalInput").ap()
        self.dnt = nc.dram_tensor("dnt", [NCH, 128, HB * 128], F32, kind="ExternalInput").ap()
        self.wqt = nc.dram_tensor("wqt", [NCH, 128, NCH * 128], F32, kind="ExternalInput").ap()
        self.wkt = nc.dram_tensor("wkt", [NCH, 128, NCH * 128], F32, kind="ExternalInput").ap()
        self.wot = nc.dram_tensor("wot", [NCH, 128, NCH * 128], F32, kind="ExternalInput").ap()
        self.wvt = nc.dram_tensor("wvt", [4, 128, NCH * 512], F32, kind="ExternalInput").ap()
        self.upsc = nc.dram_tensor("upsc", [HB, 128], F32, kind="ExternalInput").ap()
        self.dsc = nc.dram_tensor("dsc", [NCH, 128], F32, kind="ExternalInput").ap()
        self.osc = nc.dram_tensor("osc", [1, 1], F32, kind="ExternalInput").ap()
        self.eye = nc.dram_tensor("eye", [128, 128], F32, kind="ExternalInput").ap()
        self.y = nc.dram_tensor("y", [D, SC], F32, kind="ExternalOutput").ap()

        self.r1 = nc.dram_tensor("r1", [D, SC], F32, kind="Internal").ap()
        self.r2 = nc.dram_tensor("r2", [D, SC], F32, kind="Internal").ap()
        self.upsgn = nc.dram_tensor("upsgn", [HB, 128, NCH * 128], BF16, kind="Internal").ap()
        self.dnsgn = nc.dram_tensor("dnsgn", [NCH, 128, HB * 128], BF16, kind="Internal").ap()
        self.wqb = nc.dram_tensor("wqb", [NCH, 128, NCH * 128], BF16, kind="Internal").ap()
        self.wkb = nc.dram_tensor("wkb", [NCH, 128, NCH * 128], BF16, kind="Internal").ap()
        self.wob = nc.dram_tensor("wob", [NCH, 128, NCH * 128], BF16, kind="Internal").ap()
        self.wvb = nc.dram_tensor("wvb", [4, 128, NCH * 512], BF16, kind="Internal").ap()
        self.qd = nc.dram_tensor("qd", [NH, 128, SC], BF16, kind="Internal").ap()
        self.atd = nc.dram_tensor("atd", [NH, 128, SC], BF16, kind="Internal").ap()
        self.kct = nc.dram_tensor("kct", [D, SC], BF16, kind="Internal")
        self.vct = [nc.dram_tensor(f"vct{i}", [SC, 512], BF16, kind="Internal") for i in range(4)]
        self.kg = nc.dram_tensor("kg", [4 * D, SC], BF16, kind="Internal")
        self.vg = [nc.dram_tensor(f"vg{i}", [4 * SC, 512], BF16, kind="Internal") for i in range(4)]

    def op(self, engine, fn, waits=(), inc=None, amt=None):
        real = []
        for w in waits:
            if w is None:
                continue
            st, v = w
            if v is None or v <= 0:
                continue
            if self.waited[engine].get(st, 0) >= v:
                continue
            self.waited[engine][st] = v
            real.append((self.sems[st], v))
        inc_sem, inc_amt, after = None, 0, None
        if inc is not None:
            inc_amt = amt if amt is not None else (16 if inc in DMA_SEMS else 1)
            self.cnt[inc] += inc_amt
            inc_sem = self.sems[inc]
            after = self.cnt[inc]

        def run(eng, real=real, fn=fn, inc_sem=inc_sem, inc_amt=inc_amt):
            for sem, v in real:
                eng.wait_ge(sem, v)
            ins = fn(eng)
            if inc_sem is not None:
                ins.then_inc(inc_sem, inc_amt)
        self.ops[engine].append(run)
        return after

    def g(self, key, default=0):
        return self.m.get(key, default)

    def build(self):
        nc = self.nc
        with ExitStack() as ctx:
            def sb(name, shape, dt):
                return ctx.enter_context(nc.sbuf_tensor(name, shape, dt))
            s = self
            s.arenaA = sb("arenaA", [128, 32768], BF16)
            s.arenaB = sb("arenaB", [128, 32768], BF16)
            s.rtbuf = sb("rtbuf", [128, 2 * 512], F32)
            s.colA = sb("colA", [128, 3 * 2048], BF16)
            s.colB = sb("colB", [128, 3 * 4096], BF16)
            s.xcast = s.arenaB[:, 16384:24576]
            s.xcast_at = s.arenaA[:, 0:8192]
            s.x2 = sb("x2", [128, 512], BF16)
            s.ring = sb("ring", [128, 10 * 512], BF16)
            s.acc = sb("acc", [128, 512], F32)
            s.att = sb("att", [128, 2 * 512], BF16)
            s.rbs = sb("rbs", [128, 512], F32)
            s.qtl = sb("qtl", [128, 2 * 512], BF16)
            s.kev = s.att
            s.rows = sb("rows", [1, 2560], F32)
            s.onesb = sb("onesb", [128, 1], BF16)
            s.onesf = sb("onesf", [128, 1], F32)
            s.onesr = sb("onesr", [1, 128], F32)
            s.eyesb = sb("eyesb", [128, 128], F32)
            s.upsn = sb("upsn", [HB, 128], F32)
            s.dscn = sb("dscn", [NCH, 128], F32)
            s.upsT = sb("upsT", [128, HB], F32)
            s.dsef = sb("dsef", [128, NCH], F32)
            s.oscb = sb("oscb", [1, 1], F32)
            s.osb = sb("osb", [128, 1], F32)
            s.ftile = sb("ftile", [128, 2 * 512], F32)
            s.consts = sb("consts", [128, 16], F32)

            s.ps_big = ctx.enter_context(nc.psum_tensor("ps_big", [128, 2048], F32))
            s.ps_a0 = ctx.enter_context(nc.psum_tensor("ps_a0", [128, 512], F32))
            s.ps_a1 = ctx.enter_context(nc.psum_tensor("ps_a1", [128, 512], F32))
            s.ps_x = ctx.enter_context(nc.psum_tensor("ps_x", [128, 512], F32))
            s.ps_y = ctx.enter_context(nc.psum_tensor("ps_y", [128, 512], F32))
            s.ps_av = [s.ps_a0, s.ps_a1]

            for st in ALL_SEMS:
                self.sems[st] = ctx.enter_context(nc.semaphore(st))
            block = ctx.enter_context(nc.Block())

            self.prologue()
            if "ffn1" in PHASES:
                only = PHASES == ("ffn1",)
                self.ffn("f1", self.xT, self.y if only else self.r1, first=True)
            if "attn" in PHASES:
                src = self.r1 if "ffn1" in PHASES else self.xT
                only = "ffn2" not in PHASES
                self.qkv(src)
                self.attention()
                self.ophase(src, self.y if only else self.r2)
            if "ffn2" in PHASES:
                src = self.r2 if "attn" in PHASES else (
                    self.r1 if "ffn1" in PHASES else self.xT)
                self.ffn("f2", src, self.y, first=False)

            final = [(st, self.cnt[st]) for st in DMA_SEMS]
            shapes = {"qd": ([NH, 128, SC], BF16), "atd": ([NH, 128, SC], BF16),
                      "kct": ([D, SC], BF16), "vct": ([SC, D], BF16),
                      "kg": ([4 * D, SC], BF16), "vg": ([4 * SC, D], BF16),
                      "r1": ([D, SC], F32), "r2": ([D, SC], F32)}
            for nm in DUMP:
                shp, dt = shapes[nm]
                dump = nc.dram_tensor("dump_" + nm, shp, dt, kind="ExternalOutput").ap()
                srcap = getattr(self, nm)
                if "Handle" in type(srcap).__name__:
                    srcap = srcap.ap()
                def fdmp(e, dump=dump, srcap=srcap):
                    return e.dma_start(out=dump, in_=srcap)
                final = [("sy", self.op("sync", fdmp, waits=final, inc="sy"))] 
            self.op("sync", lambda e: e.nop(), waits=final + [(st, self.cnt[st]) for st in DMA_SEMS])

            @block.sync
            def _(e):
                for f in self.ops["sync"]:
                    f(e)

            @block.gpsimd
            def _(e):
                for f in self.ops["gpsimd"]:
                    f(e)

            @block.tensor
            def _(e):
                for f in self.ops["tensor"]:
                    f(e)

            @block.scalar
            def _(e):
                for f in self.ops["scalar"]:
                    f(e)

            @block.vector
            def _(e):
                for f in self.ops["vector"]:
                    f(e)
        return nc

    def prologue(self):
        s, m = self, self.m
        for idx, val in enumerate([EPS, -0.5, SCALE, MBIAS, -1.0, 0.5, 1.0 / D, 1.0, 0.0]):
            s.op("vector", lambda v, idx=idx, val=val: v.memset(
                s.consts[:, idx:idx + 1], val))
            self.nc.const_aps.register(F32, val, s.consts[:, idx:idx + 1])
        s.op("vector", lambda v: v.memset(s.onesb[:, :], 1.0))
        s.op("vector", lambda v: v.memset(s.onesf[:, :], 1.0))
        m["ones"] = s.op("vector", lambda v: v.memset(s.onesr[:, :], 1.0), inc="dve")
        s.op("sync", lambda e: e.dma_start(out=s.eyesb[:, :], in_=s.eye[:, :]), inc="sy")
        s.op("sync", lambda e: e.dma_start(out=s.dscn[:, :], in_=s.dsc[:, :]), inc="sy")
        s.op("sync", lambda e: e.dma_start(out=s.upsn[:, :], in_=s.upsc[:, :]), inc="sy")
        ld = s.op("sync", lambda e: e.dma_start(out=s.oscb[:, :], in_=s.osc[:, :]), inc="sy")
        m["upsT_mm"] = s.op(
            "tensor",
            lambda t: t.matmul(s.ps_x[:, 0:HB], s.upsn[:, :], s.eyesb[0:64, 0:HB],
                               start=True, stop=True),
            waits=[("sy", ld)], inc="pe")
        m["dsT_mm"] = s.op(
            "tensor",
            lambda t: t.matmul(s.ps_y[:, 0:NCH], s.dscn[:, :], s.eyesb[0:NCH, 0:NCH],
                               start=True, stop=True),
            inc="pe")
        m["upsT"] = s.op("scalar", lambda e: e.copy(s.upsT[:, :], s.ps_x[:, 0:HB]),
                         waits=[("pe", m["upsT_mm"])], inc="act")
        m["dsT"] = s.op("scalar", lambda e: e.copy(s.dsef[:, :], s.ps_y[:, 0:NCH]),
                        waits=[("pe", m["dsT_mm"])], inc="act")
        m["osb_mm"] = s.op(
            "tensor", lambda t: t.matmul(s.ps_x[:, 0:1], s.onesr[:, :], s.oscb[:, :],
                                         start=True, stop=True),
            waits=[("dve", m["ones"]), ("act", m["upsT"])], inc="pe")
        m["osb"] = s.op("scalar", lambda e: e.copy(s.osb[:, :], s.ps_x[:, 0:1]),
                        waits=[("pe", m["osb_mm"])], inc="act")
        m["dsef"] = s.op(
            "vector",
            lambda v: v.tensor_scalar(s.dsef[:, :], s.dsef[:, :], s.osb[:, 0:1], 0.5,
                                      ALU.mult, ALU.mult),
            waits=[("act", m["osb"]), ("act", m["dsT"])], inc="dve")
        m["psx_free"] = ("act", m["osb"])
        m["psy_free"] = ("act", m["dsT"])
        m[("xcast_free", "B")] = ("dve", 0)
        m[("xcast_free", "A")] = ("pe", 0)
        m["x2_free"] = ("pe", 0)
        m["rows_free"] = ("pe", 0)

    def ln_tile(self, tag, tt, src, src_ready=(), xn_war=None, slot=None, xbuf="B"):
        s, m = self, self.m
        slot = tt % 2 if slot is None else slot
        xn = s.arenaB[:, slot * 8192:(slot + 1) * 8192]
        xcast = s.xcast if xbuf == "B" else s.xcast_at
        gpw = [m[("xcast_free", xbuf)]] + list(src_ready)
        for gq in range(4):
            def f(e, gq=gq, tt=tt, src=src):
                return e.dma_start(
                    out=xcast[:, gq * 2048:(gq + 1) * 2048].rearrange("p (c t) -> p c t", c=4),
                    in_=src[gq * 512:(gq + 1) * 512, tt * TT:(tt + 1) * TT]
                        .rearrange("(c p) t -> p c t", p=128))
            xc = s.op("gpsimd", f, waits=gpw if gq == 0 else (), inc="xc")
        for c in range(NCH):
            def f(t, c=c):
                return t.matmul(s.ps_x[0:1, :], s.onesb[:, :],
                                xcast[:, c * 512:(c + 1) * 512],
                                start=(c == 0), stop=(c == NCH - 1), skip_group_check=True)
            r = s.op("tensor", f, waits=[("xc", xc), m["psx_free"]] if c == 0 else (),
                     inc="pe" if c == NCH - 1 else None)
        mm_sum = r
        for c in range(NCH):
            def fs(e, c=c):
                return e.activation(s.x2[:, :], xcast[:, c * 512:(c + 1) * 512],
                                    AF.Square)
            aw = [("xc", xc), m["x2_free"]] + ([m["psy_free"]] if c == 0 else [])
            sq = s.op("scalar", fs, waits=aw, inc="act")
            def f(t, c=c):
                return t.matmul(s.ps_y[0:1, :], s.onesb[:, :], s.x2[:, :],
                                start=(c == 0), stop=(c == NCH - 1), skip_group_check=True)
            r = s.op("tensor", f, waits=[("act", sq)], inc="pe")
            m["x2_free"] = ("pe", r)
        mm_ss = r
        mr = s.rows[0:1, 0:512]
        vr = s.rows[0:1, 512:1024]
        m2 = s.rows[0:1, 1024:1536]
        rr = s.rows[0:1, 1024:1536]
        s.op("vector", lambda v: v.tensor_scalar_mul(mr, s.ps_x[0:1, :], 1.0 / D),
             waits=[("pe", mm_sum), m["rows_free"]])
        s.op("vector", lambda v: v.tensor_scalar_mul(vr, s.ps_y[0:1, :], 1.0 / D),
             waits=[("pe", mm_ss)])
        s.op("vector", lambda v: v.tensor_mul(m2, mr, mr))
        rowv = s.op("vector", lambda v: v.tensor_sub(vr, vr, m2), inc="dve")
        s.op("scalar", lambda e: e.activation(vr, vr, AF.Ln, bias=EPS),
             waits=[("dve", rowv)])
        rowr = s.op("scalar", lambda e: e.activation(rr, vr, AF.Exp, scale=-0.5), inc="act")
        mb = s.op("tensor",
                  lambda t: t.matmul(s.ps_x[:, :], s.onesr[:, :], mr,
                                     start=True, stop=True, skip_group_check=True),
                  waits=[("dve", rowv)], inc="pe")
        rb = s.op("tensor",
                  lambda t: t.matmul(s.ps_y[:, :], s.onesr[:, :], rr,
                                     start=True, stop=True, skip_group_check=True),
                  waits=[("act", rowr)], inc="pe")
        m["rows_free"] = ("pe", rb)
        dw = [("pe", mb), ("pe", self.g(("xnfree", slot))), xn_war]
        for c in range(NCH):
            def f(v, c=c):
                return v.tensor_sub(xn[:, c * 512:(c + 1) * 512],
                                    xcast[:, c * 512:(c + 1) * 512], s.ps_x[:, :])
            r = s.op("vector", f, waits=dw if c == 0 else (),
                     inc="dve" if c == NCH - 1 else None)
        m[("xcast_free", xbuf)] = ("dve", r)
        m["psx_free"] = ("dve", r)
        for c in range(NCH):
            def f(v, c=c):
                return v.tensor_mul(xn[:, c * 512:(c + 1) * 512],
                                    xn[:, c * 512:(c + 1) * 512], s.ps_y[:, :])
            r = s.op("vector", f, waits=[("pe", rb)] if c == 0 else (),
                     inc="dve" if c == NCH - 1 else None)
        m["psy_free"] = ("dve", r)
        m[(tag, "xn", tt)] = r

    def ffn(self, tag, src, dst, first):
        s, m = self, self.m
        self.ln_tile(tag, 0, src, src_ready=self.g((tag, "src_ready", 0), ()),
                     xn_war=self.g("xn_war_" + tag, None))

        def conv_up(jp):
            qsl = jp % 3
            semb = f"cb{qsl}"
            csl = s.colB[:, qsl * 4096:(qsl + 1) * 4096]
            lw = [("pe", self.g(("colB", qsl))), (semb, self.g(("wb", semb)))]
            gl = s.op("gpsimd",
                      lambda e, jp=jp, csl=csl: e.dma_start(
                          out=csl.rearrange("p (c t) -> p c t", c=2),
                          in_=s.upt[2 * jp:2 * jp + 2].rearrange("c p t -> p c t")),
                      waits=lw, inc=semb)
            sg = s.op("scalar", lambda e, csl=csl: e.activation(csl, csl, AF.Sign),
                      waits=[(semb, gl)], inc="act")
            wb = s.op(
                "sync", lambda e, jp=jp, csl=csl: e.dma_start(
                    out=s.upsgn[2 * jp:2 * jp + 2].rearrange("c p t -> p c t"),
                    in_=csl.rearrange("p (c t) -> p c t", c=2)),
                waits=[("act", sg)], inc=semb)
            m[("upwb", 2 * jp)] = (semb, wb)
            m[("upwb", 2 * jp + 1)] = (semb, wb)
            m[("wb", semb)] = wb
            m[("upcv", 2 * jp)] = ("act", sg)
            m[("upcv", 2 * jp + 1)] = ("act", sg)

        def conv_dn(hidx):
            i, hf = divmod(hidx, 2)
            qsl = hidx % 3
            semb = f"cb{qsl}"
            csl = s.colB[:, qsl * 4096:(qsl + 1) * 4096]
            lw = [("pe", self.g(("colB", qsl))), (semb, self.g(("wb", semb)))]
            gl = s.op("gpsimd",
                      lambda e, i=i, hf=hf, csl=csl: e.dma_start(
                          out=csl, in_=s.dnt[i][:, hf * 4096:(hf + 1) * 4096]),
                      waits=lw, inc=semb)
            sg = s.op("scalar",
                      lambda e, csl=csl: e.activation(csl, csl, AF.Sign),
                      waits=[(semb, gl)], inc="act")
            m[("dnwb", i, hf)] = (semb, s.op(
                "sync", lambda e, i=i, hf=hf, csl=csl: e.dma_start(
                    out=s.dnsgn[i][:, hf * 4096:(hf + 1) * 4096], in_=csl),
                waits=[("act", sg)], inc=semb))
            m[("wb", semb)] = m[("dnwb", i, hf)][1]
            m[("dncv", hidx)] = ("act", sg)

        for tt in range(NTT):
            xn = s.arenaB[:, (tt % 2) * 8192:(tt % 2 + 1) * 8192]
            if first and tt == 0:
                conv_up(0)
            for j in range(HB):
                if first and tt == 0:
                    qsl = (j // 2) % 3
                    csl = s.colB[:, qsl * 4096 + (j % 2) * 2048:
                                 qsl * 4096 + (j % 2) * 2048 + 2048]
                    wready = m[("upcv", j)]
                else:
                    sl = j % 3
                    sem = f"ca{sl}"
                    csl = s.colA[:, sl * 2048:(sl + 1) * 2048]
                    lw = [("pe", self.g(("colA", sl))), (sem, self.g(("wb", sem)))]
                    r = s.op("sync",
                             lambda e, j=j, csl=csl: e.dma_start(out=csl, in_=s.upsgn[j]),
                             waits=lw + [m[("upwb", j)]], inc=sem)
                    wready = (sem, r)
                pw = [wready, ("dve", m[(tag, "xn", tt)]),
                      self.g(("pp", j % 4), None), ("act", m["upsT"])]
                for k in range(NCH):
                    def f(t, k=k, j=j, csl=csl, xn=xn):
                        return t.matmul(s.ps_big[:, (j % 4) * 512:(j % 4 + 1) * 512],
                                        csl[:, k * 128:(k + 1) * 128],
                                        xn[:, k * 512:(k + 1) * 512],
                                        start=(k == 0), stop=(k == NCH - 1),
                                        skip_group_check=True)
                    r = s.op("tensor", f, waits=pw if k == 0 else (),
                             inc="pe" if k == NCH - 1 else None)
                if first and tt == 0:
                    if j % 2 == 1:
                        m[("colB", (j // 2) % 3)] = r
                        if (j + 1) // 2 < HB // 2:
                            conv_up((j + 1) // 2)
                else:
                    m[("colA", sl)] = r
                def fe(e, j=j):
                    return e.activation(s.arenaA[:, j * 512:(j + 1) * 512],
                                        s.ps_big[:, (j % 4) * 512:(j % 4 + 1) * 512],
                                        AF.Relu, scale=s.upsT[:, j:j + 1])
                ev = s.op("scalar", fe,
                          waits=[("pe", r), ("pe", self.g(("arenaA",)))], inc="act")
                m[("pp", j % 4)] = ("act", ev)
            m[("xnfree", tt % 2)] = r
            relu_all = ev
            if tt + 1 < NTT:
                self.ln_tile(tag, tt + 1, src,
                             src_ready=self.g((tag, "src_ready", tt + 1), ()))
            if first and tt == 0:
                conv_dn(0)
                conv_dn(1)
            for i in range(NCH):
                rsl = i % 2
                def fr(e, i=i, tt=tt, rsl=rsl, src=src):
                    return e.dma_start(
                        out=s.rtbuf[:, rsl * 512:(rsl + 1) * 512],
                        in_=src[i * 128:(i + 1) * 128, tt * TT:(tt + 1) * TT])
                rtw = [("dve", self.g(("rtbuf", rsl)))] + \
                    list(self.g((tag, "src_ready", tt), ()))
                rt = s.op("gpsimd", fr, waits=rtw, inc=f"rt{rsl}")
                gw = [("act", relu_all), self.g(("pp", i % 4), None)]
                for hf in range(2):
                    qsl = (i * 2 + hf) % 3
                    semb = f"cb{qsl}"
                    csl = s.colB[:, qsl * 4096:(qsl + 1) * 4096]
                    lw = [("pe", self.g(("colB", qsl))), (semb, self.g(("wb", semb)))]
                    if first and tt == 0:
                        wready = m[("dncv", i * 2 + hf)]
                    else:
                        r = s.op("sync",
                                 lambda e, i=i, hf=hf, csl=csl: e.dma_start(
                                     out=csl, in_=s.dnsgn[i][:, hf * 4096:(hf + 1) * 4096]),
                                 waits=lw + [m[("dnwb", i, hf)]], inc=semb)
                        wready = (semb, r)
                    for k4 in range(32):
                        c = hf * 32 + k4
                        def f(t, k4=k4, c=c, i=i, csl=csl):
                            return t.matmul(
                                s.ps_big[:, (i % 4) * 512:(i % 4 + 1) * 512],
                                csl[:, k4 * 128:(k4 + 1) * 128],
                                s.arenaA[:, c * 512:(c + 1) * 512],
                                start=(c == 0), stop=(c == HB - 1),
                                skip_group_check=True)
                        w = (gw + [wready]) if k4 == 0 else ()
                        r = s.op("tensor", f, waits=w,
                                 inc="pe" if k4 == 31 else None)
                    m[("colB", qsl)] = r
                if first and tt == 0:
                    for nh in (i * 2 + 2, i * 2 + 3):
                        if nh < 2 * NCH:
                            conv_dn(nh)
                def fa(v, i=i, rsl=rsl):
                    return v.scalar_tensor_tensor(
                        s.ftile[:, (i % 2) * 512:(i % 2 + 1) * 512],
                        s.ps_big[:, (i % 4) * 512:(i % 4 + 1) * 512],
                        s.dsef[:, i:i + 1],
                        s.rtbuf[:, rsl * 512:(rsl + 1) * 512],
                        ALU.mult, ALU.add)
                ra = s.op("vector", fa,
                          waits=[("pe", r), (f"ft{i % 2}", self.g(("ftw", i % 2))),
                                 ("dve", m["dsef"]), (f"rt{rsl}", rt)], inc="dve")
                m[("pp", i % 4)] = ("dve", ra)
                m[("rtbuf", rsl)] = ra
                def fw(e, i=i, tt=tt, dst=dst):
                    return e.dma_start(out=dst[i * 128:(i + 1) * 128, tt * TT:(tt + 1) * TT],
                                       in_=s.ftile[:, (i % 2) * 512:(i % 2 + 1) * 512])
                m[("ftw", i % 2)] = s.op("gpsimd", fw, waits=[("dve", ra)], inc=f"ft{i % 2}")
            m[("arenaA",)] = r
            m[(tag, "dnw", tt)] = (("ft0", self.cnt["ft0"]), ("ft1", self.cnt["ft1"]))

    def qkv(self, src):
        s, m = self, self.m
        if ("f1", "dnw", 0) in m:
            for tt in range(NTT):
                m[("at", "src_ready", tt)] = m[("f1", "dnw", tt)]
        m[("xcast_free", "A")] = ("pe", self.g(("arenaA",)))
        for tt in range(NTT):
            war = m[("xcast_free", "B")] if tt == 2 else None
            self.ln_tile("at", tt, src, src_ready=self.g(("at", "src_ready", tt), ()),
                         slot=tt, xbuf="B" if tt < 2 else "A", xn_war=war)

        def emit_kq(which):
            wt = s.wkt if which == "k" else s.wqt
            for j in range(NCH):
                sl = j % 3
                sem = f"ca{sl}"
                csl = s.colA[:, sl * 2048:(sl + 1) * 2048]
                gl = s.op("gpsimd",
                          lambda e, j=j, csl=csl, wt=wt: e.dma_start(out=csl, in_=wt[j]),
                          waits=[("pe", self.g(("colA", sl)))], inc=sem)
                for tt in range(NTT):
                    xn = s.arenaB[:, tt * 8192:(tt + 1) * 8192]
                    bank = (j * 4 + tt) % 4
                    pw = [(sem, gl), ("dve", m[("at", "xn", tt)]),
                          self.g(("pp", bank), None)]
                    for k in range(NCH):
                        def f(t, k=k, csl=csl, xn=xn, bank=bank):
                            return t.matmul(
                                s.ps_big[:, bank * 512:(bank + 1) * 512],
                                csl[:, k * 128:(k + 1) * 128],
                                xn[:, k * 512:(k + 1) * 512],
                                start=(k == 0), stop=(k == NCH - 1),
                                skip_group_check=True)
                        r = s.op("tensor", f, waits=pw if k == 0 else (),
                                 inc="pe" if k == NCH - 1 else None)
                    m[("colA", sl)] = r
                    esl = (j * 4 + tt) % 2
                    def fe(e, bank=bank, esl=esl):
                        return e.activation(s.kev[:, esl * 512:(esl + 1) * 512],
                                            s.ps_big[:, bank * 512:(bank + 1) * 512],
                                            AF.Copy)
                    a = s.op("scalar", fe,
                             waits=[("pe", r), (f"kv{esl}", self.g(("kevw", esl)))],
                             inc="act")
                    m[("pp", bank)] = ("act", a)
                    if which == "k":
                        def fw(e, j=j, tt=tt, esl=esl):
                            return e.dma_start(
                                out=s.kct.ap()[j * 128:(j + 1) * 128, tt * TT:(tt + 1) * TT],
                                in_=s.kev[:, esl * 512:(esl + 1) * 512])
                    else:
                        def fw(e, j=j, tt=tt, esl=esl):
                            return e.dma_start(
                                out=s.qd[j][:, tt * TT:(tt + 1) * TT],
                                in_=s.kev[:, esl * 512:(esl + 1) * 512])
                    m[("kevw", esl)] = s.op("sync", fw, waits=[("act", a)],
                                            inc=f"kv{esl}")
                    if which == "q":
                        m[("qw", j, tt)] = (f"kv{esl}", m[("kevw", esl)])

        def emit_v():
            for sl4 in range(4):
                rdy = []
                for h2 in range(2):
                    qsl = (sl4 * 2 + h2) % 3
                    semb = f"cb{qsl}"
                    csl = s.colB[:, qsl * 4096:(qsl + 1) * 4096]
                    gl = s.op("gpsimd",
                              lambda e, sl4=sl4, h2=h2, csl=csl: e.dma_start(
                                  out=csl, in_=s.wvt[sl4][:, h2 * 4096:(h2 + 1) * 4096]),
                              waits=[("pe", self.g(("colB", qsl)))], inc=semb)
                    rdy.append((semb, gl))
                    m[("vcsl", h2)] = csl
                    m[("vqsl", h2)] = qsl
                for idx in range(16):
                    tt, tb = idx // 4, idx % 4
                    xn = s.arenaB[:, tt * 8192:(tt + 1) * 8192]
                    bank = idx % 4
                    pw = rdy + [("dve", m[("at", "xn", tt)]),
                                self.g(("pp", bank), None)]
                    for k in range(NCH):
                        csl = m[("vcsl", k // 8)]
                        def f(t, k=k, csl=csl, xn=xn, tb=tb, bank=bank):
                            return t.matmul(
                                s.ps_big[:, bank * 512:(bank + 1) * 512],
                                xn[:, k * 512 + tb * 128:k * 512 + (tb + 1) * 128],
                                csl[:, (k % 8) * 512:(k % 8 + 1) * 512],
                                start=(k == 0), stop=(k == NCH - 1),
                                skip_group_check=True)
                        r = s.op("tensor", f, waits=pw if k == 0 else (),
                                 inc="pe" if k == NCH - 1 else None)
                    for h2 in range(2):
                        m[("colB", m[("vqsl", h2)])] = r
                    esl = idx % 2
                    def fe(e, bank=bank, esl=esl):
                        return e.activation(s.kev[:, esl * 512:(esl + 1) * 512],
                                            s.ps_big[:, bank * 512:(bank + 1) * 512],
                                            AF.Copy)
                    a = s.op("scalar", fe,
                             waits=[("pe", r), (f"kv{esl}", self.g(("kevw", esl)))],
                             inc="act")
                    m[("pp", bank)] = ("act", a)
                    def fw(e, tb=tb, tt=tt, sl4=sl4, esl=esl):
                        return e.dma_start(
                            out=s.vct[sl4].ap()[tt * 512 + tb * 128:tt * 512 + (tb + 1) * 128, :],
                            in_=s.kev[:, esl * 512:(esl + 1) * 512])
                    m[("kevw", esl)] = s.op("sync", fw, waits=[("act", a)],
                                            inc=f"kv{esl}")
                kv_tot = [("kv0", self.cnt["kv0"]), ("kv1", self.cnt["kv1"])]
                s.op("gpsimd",
                     lambda e, sl4=sl4: e.collective_compute(
                         "AllGather", ALU.bypass, replica_groups=GROUPS,
                         ins=[s.vct[sl4].ap().opt()],
                         outs=[s.vg[sl4].ap().opt()]),
                     waits=kv_tot, inc="cc", amt=1)

        emit_kq("k")
        kv_tot = [("kv0", self.cnt["kv0"]), ("kv1", self.cnt["kv1"])]
        s.op("gpsimd",
             lambda e: e.collective_compute(
                 "AllGather", ALU.bypass, replica_groups=GROUPS,
                 ins=[s.kct.ap().opt()], outs=[s.kg.ap().opt()]),
             waits=kv_tot, inc="cc", amt=1)
        emit_v()
        emit_kq("q")
        m["q_done"] = [("kv0", self.cnt["kv0"]), ("kv1", self.cnt["kv1"])]
        m["qkv_pe_done"] = self.cnt["pe"]
        m[("arenaA",)] = self.cnt["pe"]

    def attention(self):
        s, m = self, self.m
        for j in range(NCH):
            sl = j % 3
            semw = f"ca{sl}"
            csl = s.colA[:, sl * 2048:(sl + 1) * 2048]
            gl = s.op("gpsimd",
                      lambda e, j=j, csl=csl: e.dma_start(out=csl, in_=s.wot[j]),
                      waits=[("pe", self.g(("colA", sl))), (semw, self.g(("wb", semw)))],
                      inc=semw)
            m[("colA", sl)] = self.g(("colA", sl))
            m[("wowb", j)] = (semw, s.op(
                "sync", lambda e, j=j, csl=csl, semw=semw: e.dma_start(out=s.wob[j], in_=csl),
                waits=[(semw, gl)], inc=semw))
            m[("wb", semw)] = m[("wowb", j)][1]
        ra = s.rows[0:1, 1536:2048]
        rb = s.rows[0:1, 2048:2560]
        for h in range(NH):
            hs = h % 2
            kh = s.arenaB[:, hs * 8192:(hs + 1) * 8192]
            vh = s.arenaB[:, 16384 + hs * 8192:16384 + (hs + 1) * 8192]
            khw = [("cc", 1), ("pe", self.g(("khfree", hs), m["qkv_pe_done"]))]
            for rk in range(4):
                def f(e, rk=rk, h=h, kh=kh):
                    return e.dma_start(
                        out=kh[:, rk * 2048:(rk + 1) * 2048],
                        in_=s.kg.ap()[rk * D + h * 128:rk * D + (h + 1) * 128, :])
                kh_rdy = s.op("sync", f, waits=khw if rk == 0 else (), inc=f"kh{hs}")
            vhw = [("cc", 2 + h // 4), ("pe", self.g(("vhfree", hs), m["qkv_pe_done"])),
                   m[("xcast_free", "B")] if hs == 0 else None]
            for g8 in range(8):
                def f(e, g8=g8, h=h, vh=vh):
                    return e.dma_start(
                        out=vh[:, g8 * 1024:(g8 + 1) * 1024].rearrange("p (c v) -> p c v", c=8),
                        in_=s.vg[h // 4].ap()[g8 * 1024:(g8 + 1) * 1024,
                                              (h % 4) * 128:(h % 4 + 1) * 128]
                            .rearrange("(c p) v -> p c v", p=128))
                vh_rdy = s.op("sync", f, waits=vhw if g8 == 0 else (), inc=f"vh{hs}")
            for qt in range(NTT):
                pi = h * NTT + qt
                qsl = qt % 2
                def fq(e, h=h, qt=qt, qsl=qsl):
                    return e.dma_start(out=s.qtl[:, qsl * 512:(qsl + 1) * 512],
                                       in_=s.qd[h][:, qt * TT:(qt + 1) * TT])
                qrd = s.op("sync", fq,
                           waits=[m[("qw", h, qt)],
                                  ("pe", self.g(("qtl", qsl), m["qkv_pe_done"]))],
                           inc=f"qt{qsl}")
                avp = s.ps_av[pi % 2]

                def emit_sc(gg):
                    gp2 = gg % 2
                    scw = [(f"kh{hs}", kh_rdy), (f"qt{qsl}", qrd),
                           self.g(("pp", 2 * gp2), None),
                           self.g(("pp", 2 * gp2 + 1), None)]
                    for u in range(2):
                        c = 2 * gg + u
                        def f(t, c=c, u=u, gp2=gp2, kh=kh, qsl=qsl):
                            return t.matmul(
                                s.ps_big[:, gp2 * 1024 + u * 512:gp2 * 1024 + (u + 1) * 512],
                                kh[:, c * 128:(c + 1) * 128],
                                s.qtl[:, qsl * 512:(qsl + 1) * 512],
                                start=True, stop=True, skip_group_check=True)
                        r = s.op("tensor", f, waits=scw if u == 0 else (),
                                 inc="pe" if u == 1 else None)
                    rsl = gg % 5
                    exw = [("pe", r), ("pe", self.g(("ringpe", rsl))),
                           ("dve", self.g(("ringdve", rsl)))]
                    def fe(e, gp2=gp2, rsl=rsl):
                        return e.activation(s.ring[:, rsl * 1024:(rsl + 1) * 1024],
                                            s.ps_big[:, gp2 * 1024:(gp2 + 1) * 1024],
                                            AF.Exp, bias=MBIAS, scale=SCALE)
                    a = s.op("scalar", fe, waits=exw, inc="act")
                    m[("pp", 2 * gp2)] = ("act", a)
                    m[("pp", 2 * gp2 + 1)] = ("act", a)
                    m[("ex", gg)] = a

                def emit_avsum(gg):
                    rsl = gg % 5
                    a = m[("ex", gg)]
                    avw = [("act", a), (f"vh{hs}", vh_rdy),
                           ("dve", self.g(("avfree", pi % 2)))]
                    for u in range(2):
                        c = 2 * gg + u
                        def f(t, c=c, u=u, rsl=rsl, vh=vh, avp=avp):
                            return t.matmul(avp[:, :], vh[:, c * 128:(c + 1) * 128],
                                            s.ring[:, (rsl * 2 + u) * 512:(rsl * 2 + u + 1) * 512],
                                            start=(c == 0), stop=(c == NKC - 1),
                                            skip_group_check=True)
                        r = s.op("tensor", f, waits=avw if u == 0 else (),
                                 inc="pe" if u == 1 else None)
                    m[("ringpe", rsl)] = r
                    if gg % 4 == 3:
                        fw_ = [("act", a)]
                        if gg == 3:
                            fw_.append(("dve", self.g(("psyfree_at", pi - 1))))
                            if pi == 0:
                                fw_.append(m["psy_free"])
                        for u in range(2):
                            def f(t, rsl=rsl, u=u, gg=gg):
                                return t.matmul(
                                    s.ps_y[0:1, :], s.onesb[:, :],
                                    s.ring[:, (rsl * 2 + u) * 512:(rsl * 2 + u + 1) * 512],
                                    start=(gg == 3 and u == 0), stop=False,
                                    skip_group_check=True)
                            r = s.op("tensor", f, waits=fw_ if u == 0 else (),
                                     inc="pe" if u == 1 else None)
                        m[("ringpe", rsl)] = r
                        return None
                    dw = [("act", a)]
                    if gg == 0:
                        dw.append(("pe", self.g(("fold", pi - 1))))
                        def f(v, rsl=rsl):
                            return v.tensor_add(
                                s.acc[:, :],
                                s.ring[:, (rsl * 2) * 512:(rsl * 2 + 1) * 512],
                                s.ring[:, (rsl * 2 + 1) * 512:(rsl * 2 + 2) * 512])
                        rd = s.op("vector", f, waits=dw, inc="dve")
                    else:
                        for u in range(2):
                            def f(v, rsl=rsl, u=u):
                                return v.tensor_add(
                                    s.acc[:, :], s.acc[:, :],
                                    s.ring[:, (rsl * 2 + u) * 512:(rsl * 2 + u + 1) * 512])
                            rd = s.op("vector", f, waits=dw if u == 0 else (),
                                     inc="dve" if u == 1 else None)
                    m[("ringdve", rsl)] = rd
                    return rd

                DELAY = 4
                accr = None
                for gg in range(NWG):
                    emit_sc(gg)
                    if gg >= DELAY:
                        rd = emit_avsum(gg - DELAY)
                        accr = rd if rd is not None else accr
                for gg in range(NWG - DELAY, NWG):
                    rd = emit_avsum(gg)
                    accr = rd if rd is not None else accr
                m[("qtl", qsl)] = self.cnt["pe"]
                tailw = [("dve", accr)]
                def ff(t):
                    return t.matmul(s.ps_y[0:1, :], s.onesf[:, :], s.acc[:, :],
                                    start=False, stop=True, skip_group_check=True)
                fold = s.op("tensor", ff, waits=tailw, inc="pe")
                m[("fold", pi)] = fold
                s.op("scalar", lambda e: e.activation(ra, s.ps_y[0:1, :], AF.Ln),
                     waits=[("pe", fold), ("pe", self.g(("rbmm", pi - 1)))])
                rcp = s.op("scalar", lambda e: e.activation(rb, ra, AF.Exp, scale=-1.0),
                           inc="act")
                rbmm = s.op("tensor",
                            lambda t: t.matmul(s.ps_y[:, :], s.onesr[:, :], rb,
                                               start=True, stop=True, skip_group_check=True),
                            waits=[("act", rcp)], inc="pe")
                m[("rbmm", pi)] = rbmm
                def fc(v, qsl=qsl):
                    return v.tensor_copy(s.rbs[:, 0:512], s.ps_y[:, :])
                rcpr = s.op("vector", fc, waits=[("pe", rbmm)], inc="dve")
                m[("psyfree_at", pi)] = rcpr
                def fm2(v, qsl=qsl, avp=avp):
                    return v.tensor_mul(s.att[:, qsl * 512:(qsl + 1) * 512],
                                        avp[:, :], s.rbs[:, 0:512])
                attm = s.op("vector", fm2,
                            waits=[("pe", m[("ringpe", (NWG - 1) % 4)]),
                                   ("dve", rcpr),
                                   (f"aw{qsl}", self.g(("atw_s", qsl)))],
                            inc="dve")
                m[("avfree", pi % 2)] = attm
                def fw2(e, h=h, qt=qt, qsl=qsl):
                    return e.dma_start(out=s.atd[h][:, qt * TT:(qt + 1) * TT],
                                       in_=s.att[:, qsl * 512:(qsl + 1) * 512])
                m[("atw_s", qsl)] = s.op("gpsimd", fw2, waits=[("dve", attm)], inc=f"aw{qsl}")
                m[("atw", pi)] = (f"aw{qsl}", m[("atw_s", qsl)])
            m[("khfree", hs)] = self.cnt["pe"]
            m[("vhfree", hs)] = self.cnt["pe"]
        m["attn_pe_done"] = self.cnt["pe"]
        m["psy_free"] = ("dve", m[("psyfree_at", NH * NTT - 1)])
        m[("xnfree", 0)] = m["attn_pe_done"]
        m[("xnfree", 1)] = m["attn_pe_done"]
        m[("xcast_free", "B")] = ("pe", m["attn_pe_done"])

    def ophase(self, rsrc, dst):
        s, m = self, self.m
        for tt in range(NTT):
            for k in range(NCH):
                def f(e, k=k, tt=tt):
                    return e.dma_start(out=s.arenaA[:, k * 512:(k + 1) * 512],
                                       in_=s.atd[k][:, tt * TT:(tt + 1) * TT])
                w = [m[("atw", k * NTT + tt)], ("pe", self.g(("arenaA",)))]
                oa = s.op("sync", f, waits=w, inc="oa")
            for j in range(NCH):
                sl = j % 2
                sem = f"ca{sl}"
                csl = s.colA[:, sl * 2048:(sl + 1) * 2048]
                lw = [("pe", self.g(("colA", sl))), m[("wowb", j)]]
                wrd = s.op("sync",
                           lambda e, j=j, csl=csl: e.dma_start(out=csl, in_=s.wob[j]),
                           waits=lw, inc=sem)
                pw = [(sem, wrd), ("oa", oa), self.g(("pp", j % 4), None)]
                for k in range(NCH):
                    def f(t, k=k, j=j, csl=csl):
                        return t.matmul(s.ps_big[:, (j % 4) * 512:(j % 4 + 1) * 512],
                                        csl[:, k * 128:(k + 1) * 128],
                                        s.arenaA[:, k * 512:(k + 1) * 512],
                                        start=(k == 0), stop=(k == NCH - 1),
                                        skip_group_check=True)
                    r = s.op("tensor", f, waits=pw if k == 0 else (),
                             inc="pe" if k == NCH - 1 else None)
                m[("colA", sl)] = r
                rsl = j % 2
                def frt(e, j=j, tt=tt, rsl=rsl, rsrc=rsrc):
                    return e.dma_start(out=s.rtbuf[:, rsl * 512:(rsl + 1) * 512],
                                       in_=rsrc[j * 128:(j + 1) * 128, tt * TT:(tt + 1) * TT])
                rtw = [("dve", self.g(("rtbuf", rsl)))] + \
                    list(self.g(("f1", "dnw", tt), ()))
                rt = s.op("gpsimd", frt, waits=rtw, inc=f"rt{rsl}")
                def fa(v, j=j, rsl=rsl):
                    return v.tensor_add(s.ftile[:, (j % 2) * 512:(j % 2 + 1) * 512],
                                        s.ps_big[:, (j % 4) * 512:(j % 4 + 1) * 512],
                                        s.rtbuf[:, rsl * 512:(rsl + 1) * 512])
                ra2 = s.op("vector", fa,
                           waits=[("pe", r), (f"ft{j % 2}", self.g(("ftw", j % 2))),
                                  (f"rt{rsl}", rt)], inc="dve")
                m[("pp", j % 4)] = ("dve", ra2)
                m[("rtbuf", rsl)] = ra2
                def fw(e, j=j, tt=tt, dst=dst):
                    return e.dma_start(out=dst[j * 128:(j + 1) * 128, tt * TT:(tt + 1) * TT],
                                       in_=s.ftile[:, (j % 2) * 512:(j % 2 + 1) * 512])
                m[("ftw", j % 2)] = s.op("gpsimd", fw, waits=[("dve", ra2)], inc=f"ft{j % 2}")
            m[("arenaA",)] = r
            m[("f2", "src_ready", tt)] = (("ft0", self.cnt["ft0"]),
                                          ("ft1", self.cnt["ft1"]))
        m["xn_war_f2"] = ("pe", self.g("attn_pe_done"))


def host_layout(inputs):
    x = np.asarray(inputs["L0_attn_x"], np.float32)

    def lhsT_tiles(W, J, K):
        W = np.asarray(W, np.float32)
        return np.ascontiguousarray(
            W.reshape(J, 128, K, 128).transpose(0, 3, 2, 1).reshape(J, 128, K * 128))

    wq = lhsT_tiles(inputs["L0_attn_w_q"], NCH, NCH)
    wk = lhsT_tiles(inputs["L0_attn_w_k"], NCH, NCH)
    wo = lhsT_tiles(inputs["L0_attn_w_o"], NCH, NCH)
    up = lhsT_tiles(inputs["L0_ffn_up_weight"], HB, NCH)
    dn = lhsT_tiles(inputs["L0_ffn_down_weight"], NCH, HB)
    wv_ = np.asarray(inputs["L0_attn_w_v"], np.float32)
    wv = np.ascontiguousarray(
        wv_.reshape(4, 512, NCH, 128).transpose(0, 3, 2, 1).reshape(4, 128, NCH * 512))
    upsc = np.ascontiguousarray(
        np.asarray(inputs["L0_ffn_up_scale"], np.float32).reshape(HB, 128))
    dsc = np.ascontiguousarray(
        np.asarray(inputs["L0_ffn_down_scale"], np.float32).reshape(NCH, 128))
    osc = np.asarray(inputs["L0_ffn_output_scale"], np.float32).reshape(1, 1)
    eye = np.eye(128, dtype=np.float32)
    shared = dict(upt=up, dnt=dn, wqt=wq, wkt=wk, wot=wo, wvt=wv,
                  upsc=upsc, dsc=dsc, osc=osc, eye=eye)
    in_maps = []
    for c in range(8):
        b, r = c // 4, c % 4
        in_maps.append(dict(shared, xT=np.ascontiguousarray(x[b, r * SC:(r + 1) * SC, :].T)))
    return in_maps


def kernel(**inputs):
    in_maps = host_layout(inputs)
    nc = Builder().build()
    res = run_bass_kernel_spmd(nc, in_maps, core_ids=list(range(8)))
    out = np.empty((2, 8192, D), np.float32)
    for c in range(8):
        b, r = c // 4, c % 4
        out[b, r * SC:(r + 1) * SC, :] = res.results[c]["y"].T
    return out
